# version 02
# baseline (speedup 1.0000x reference)
import os
import numpy as np
import concourse.bass as bass
import concourse.mybir as mybir
from concourse.bass_utils import run_bass_kernel_spmd
from concourse.bass_core_rust import IndirectOffsetOnAxis
from contextlib import ExitStack

F32 = mybir.dt.float32
BF16 = mybir.dt.bfloat16
I32 = mybir.dt.int32
AF = mybir.ActivationFunctionType
ALU = mybir.AluOpType
AX = mybir.AxisListType.X

D = 4096
T = 1024
L = 4096
NHL = 8
HD = 128
NOC = D // 128
NDC = D // 128
NE = 16
RK = 64
GROUPS = [[0, 1, 2, 3], [4, 5, 6, 7]]
SM_SCALE = float(1.0 / np.sqrt(HD))
NEG_BIG = -1.0e30


class S:

    def __init__(self, nc, es, name):
        ec = es.enter_context if hasattr(es, "enter_context") else es
        self.h = ec(nc.semaphore(name))
        self.n = 0

    def inc(self, instr, k=1):
        instr.then_inc(self.h, k)
        self.n += k
        return self.n

    def wait(self, eng, tok=None):
        tok = self.n if tok is None else tok
        if tok > 0:
            eng.wait_ge(self.h, tok)


def slot_sems(nc, es, name, n):
    return [S(nc, es, f"{name}{i}") for i in range(n)]


def build_nc(phases="ABC", debug=()):
    nc = bass.Bass("TRN2", target_bir_lowering=False, debug=False)

    x_fm = nc.dram_tensor("x_fm", [D, T], F32, kind="ExternalInput").ap()
    wq_t = nc.dram_tensor("wq_t", [NOC, 128, NDC, 128], F32, kind="ExternalInput").ap()
    wk_t = nc.dram_tensor("wk_t", [NOC, 128, NDC, 128], F32, kind="ExternalInput").ap()
    wv_t = nc.dram_tensor("wv_t", [NOC, 128, NDC, 128], F32, kind="ExternalInput").ap()
    wo_t = nc.dram_tensor("wo_t", [NOC, 128, NDC, 128], F32, kind="ExternalInput").ap()
    wb_t = nc.dram_tensor("wb_t", [NOC, 128, NDC, 128], F32, kind="ExternalInput").ap()
    a_t = nc.dram_tensor("a_t", [NDC, 128, RK], F32, kind="ExternalInput").ap()
    r_t = nc.dram_tensor("r_t", [NDC, 128, NE], F32, kind="ExternalInput").ap()
    b_t = nc.dram_tensor("b_t", [NOC, NE, RK, 128], F32, kind="ExternalInput").ap()
    bias5 = nc.dram_tensor("bias5", [128, 5 * NOC], F32, kind="ExternalInput").ap()
    idx_t = nc.dram_tensor("idx_t", [128, 32], I32, kind="ExternalInput").ap()

    y = nc.dram_tensor("y", [D, T], F32, kind="ExternalOutput").ap()
    dbg = {}
    dbg_shapes = {"d_k": ([D, T], F32), "d_q": ([D, T], F32),
                  "d_v": ([D, T], F32), "kg_d": ([4 * D, T], F32),
                  "ao_d": ([D, L], F32), "x2_d": ([D, T], F32),
                  "xn2_d": ([D, T], F32)}
    for name in debug:
        sh, dt = dbg_shapes[name]
        dbg[name] = nc.dram_tensor(name, sh, dt, kind="ExternalOutput").ap()

    cin_k = nc.dram_tensor("cin_k", [D, T], BF16, kind="Internal").ap()
    cin_q = nc.dram_tensor("cin_q", [D, T], BF16, kind="Internal").ap()
    cin_v = nc.dram_tensor("cin_v", [D, T], BF16, kind="Internal").ap()
    kg = nc.dram_tensor("kg", [4 * D, T], BF16, kind="Internal").ap()
    qg = nc.dram_tensor("qg", [4 * D, T], BF16, kind="Internal").ap()
    vg = nc.dram_tensor("vg", [4 * D, T], BF16, kind="Internal").ap()
    cin_ao = nc.dram_tensor("cin_ao", [D, T], BF16, kind="Internal").ap()
    aog = nc.dram_tensor("aog", [4 * D, T], BF16, kind="Internal").ap()
    x2sp = nc.dram_tensor("x2sp", [D, T], F32, kind="Internal").ap()

    es_glob = ExitStack()
    e = es_glob.enter_context
    with es_glob:
        wbuf = [e(nc.sbuf_tensor(f"wbuf{i}", [128, D], BF16)) for i in range(2)]
        stage = [e(nc.sbuf_tensor(f"stage{i}", [128, T], BF16)) for i in range(2)]
        fstage = [e(nc.sbuf_tensor(f"fstage{i}", [128, T], F32)) for i in range(2)]
        bias_sb = e(nc.sbuf_tensor("bias_sb", [128, 5 * NOC], F32))
        idx_sb = e(nc.sbuf_tensor("idx_sb", [128, 32], I32))
        ones_bf = e(nc.sbuf_tensor("ones_bf", [128, 128], BF16))
        iden_bf = e(nc.sbuf_tensor("iden_bf", [128, 128], BF16))
        iota_sc = e(nc.sbuf_tensor("iota_sc", [128, 512], F32))
        masks = e(nc.sbuf_tensor("masks", [128, 4 * 512], F32))
        m1t = e(nc.sbuf_tensor("m1t", [128, 512], F32))

        SEMS = ["s_in", "s_x", "s_sq", "s_stat", "s_v", "s_s", "s_xn",
                "s_w", "s_mm", "s_ev", "s_stk", "s_stq", "s_stv",
                "s_cc", "s_g", "s_t", "s_sc", "s_scev", "s_soft",
                "s_pt", "s_ptev", "s_av", "s_ao", "s_aog",
                "s_x2", "s_y", "s_misc", "s_bst", "s_fst"]
        sm = {n: S(nc, es_glob, n) for n in SEMS}
        s_xs = slot_sems(nc, es_glob, "s_xs", 3)
        s_ws = slot_sems(nc, es_glob, "s_ws", 2)
        s_bsts = slot_sems(nc, es_glob, "s_bsts", 2)
        s_in = sm["s_in"]; s_x = sm["s_x"]; s_sq = sm["s_sq"]
        s_stat = sm["s_stat"]; s_v = sm["s_v"]; s_s = sm["s_s"]
        s_xn = sm["s_xn"]; s_w = sm["s_w"]; s_mm = sm["s_mm"]
        s_ev = sm["s_ev"]; s_cc = sm["s_cc"]; s_g = sm["s_g"]
        s_t = sm["s_t"]; s_sc = sm["s_sc"]; s_scev = sm["s_scev"]
        s_soft = sm["s_soft"]; s_pt = sm["s_pt"]; s_ptev = sm["s_ptev"]
        s_av = sm["s_av"]; s_ao = sm["s_ao"]; s_aog = sm["s_aog"]
        s_x2 = sm["s_x2"]; s_y = sm["s_y"]; s_misc = sm["s_misc"]
        s_bst = sm["s_bst"]; s_fst = sm["s_fst"]
        s_st = {"k": sm["s_stk"], "q": sm["s_stq"], "v": sm["s_stv"]}

        blk = e(nc.Block())

        def _sync0(sy):
            s_in.inc(sy.dma_start(out=bias_sb[:], in_=bias5[:]), 16)
            s_in.inc(sy.dma_start(out=idx_sb[:], in_=idx_t[:]), 16)
        blk.sync(_sync0)

        def _g0(g):
            s_misc.inc(g.iota(iota_sc[:], pattern=[[1, 512]], base=0,
                              channel_multiplier=-1,
                              allow_small_or_imprecise_dtypes=True))
        blk.gpsimd(_g0)

        def _v0(v):
            s_v.inc(v.memset(ones_bf[:], 1.0))
            s_misc.wait(v, 1)
            s_v.inc(v.tensor_scalar(iden_bf[:], iota_sc[:, 0:128], 0.0, None,
                                    ALU.is_equal))
            for qb in range(4):
                t1 = s_v.inc(v.tensor_scalar(m1t[:], iota_sc[:],
                                             float(-qb * 128), 0.0,
                                             ALU.add, ALU.is_gt))
                v.wait_ge(s_v.h, t1)
                t2 = s_v.inc(v.tensor_scalar(masks[:, qb * 512:(qb + 1) * 512],
                                             m1t[:], NEG_BIG, None, ALU.mult))
                v.wait_ge(s_v.h, t2)
        blk.vector(_v0)
        consts_tok = s_v.n

        esA = ExitStack()
        ea = esA.enter_context
        xch = [ea(nc.sbuf_tensor(f"xch{i}", [128, T], BF16)) for i in range(3)]
        xn1 = ea(nc.sbuf_tensor("xn1", [128, NDC * T], BF16))
        sqb = [ea(nc.sbuf_tensor(f"sqb{i}", [128, T], BF16)) for i in range(2)]
        mu_b = ea(nc.sbuf_tensor("mu_b", [128, T], F32))
        msq_b = ea(nc.sbuf_tensor("msq_b", [128, T], F32))
        nmsq_b = ea(nc.sbuf_tensor("nmsq_b", [128, T], F32))
        var_b = ea(nc.sbuf_tensor("var_b", [128, T], F32))
        lnv_b = ea(nc.sbuf_tensor("lnv_b", [128, T], F32))
        rstd_b = ea(nc.sbuf_tensor("rstd_b", [128, T], F32))
        mrb_b = ea(nc.sbuf_tensor("mrb_b", [128, T], F32))
        tnc = [ea(nc.sbuf_tensor(f"tnc{i}", [128, T], BF16)) for i in range(2)]
        ps_sum = ea(nc.psum_tensor("ps_sum", [128, T], F32))
        ps_sq = ea(nc.psum_tensor("ps_sq", [128, T], F32))
        ps_qkv = [ea(nc.psum_tensor(f"ps_qkv{i}", [128, T], F32)) for i in range(2)]

        def xnc(dc):
            return xn1[:, dc * T:(dc + 1) * T]

        x_tok = {}
        sq_tok = {}
        stat_tok = {}
        x_free = {}
        sq_free = {}
        ldw_ones = [False]

        for dc in range(NDC):
            xb = dc % 3
            sb = dc % 2

            def _gx(g, dc=dc, xb=xb):
                if dc >= 3:
                    g.wait_ge(s_stat.h, x_free[xb])
                x_tok[dc] = s_xs[xb].inc(g.dma_start(
                    out=xch[xb][:], in_=x_fm[dc * 128:(dc + 1) * 128, :]), 16)
            blk.gpsimd(_gx)

            def _vsq(v, dc=dc, xb=xb, sb=sb):
                v.wait_ge(s_xs[xb].h, x_tok[dc])
                if dc >= 2:
                    v.wait_ge(s_stat.h, sq_free[sb])
                sq_tok[dc] = s_sq.inc(v.tensor_mul(sqb[sb][:], xch[xb][:],
                                                   xch[xb][:]))
            blk.vector(_vsq)

            def _tst(t, dc=dc, xb=xb, sb=sb):
                if not ldw_ones[0]:
                    t.wait_ge(s_v.h, consts_tok)
                    t.ldweights(ones_bf[:])
                    ldw_ones[0] = True
                t.wait_ge(s_xs[xb].h, x_tok[dc])
                t.wait_ge(s_sq.h, sq_tok[dc])
                st = dc == 0
                sp = dc == NDC - 1
                for tt in range(2):
                    t.matmul(ps_sum[:, tt * 512:(tt + 1) * 512], ones_bf[:],
                             xch[xb][:, tt * 512:(tt + 1) * 512],
                             start=st, stop=sp)
                for tt in range(2):
                    ins = t.matmul(ps_sq[:, tt * 512:(tt + 1) * 512], ones_bf[:],
                                   sqb[sb][:, tt * 512:(tt + 1) * 512],
                                   start=st, stop=sp)
                stat_tok[dc] = s_stat.inc(ins)
                x_free[xb] = stat_tok[dc]
                sq_free[sb] = stat_tok[dc]
            blk.tensor(_tst)

        def _vln(v):
            v.wait_ge(s_stat.h, stat_tok[NDC - 1])
            t1 = s_v.inc(v.tensor_scalar(mu_b[:], ps_sum[:], 1.0 / D, None,
                                         ALU.mult))
            t2 = s_v.inc(v.tensor_scalar(msq_b[:], ps_sq[:], 1.0 / D, 1e-5,
                                         ALU.mult, ALU.add))
            v.wait_ge(s_v.h, t1)
            t3 = s_v.inc(v.scalar_tensor_tensor(nmsq_b[:], mu_b[:], -1.0,
                                                mu_b[:], ALU.mult, ALU.mult))
            v.wait_ge(s_v.h, t3)
            s_v.inc(v.tensor_add(var_b[:], msq_b[:], nmsq_b[:]))
        blk.vector(_vln)
        vtok_var = s_v.n

        def _sln(s):
            s.wait_ge(s_v.h, vtok_var)
            t1 = s_s.inc(s.activation(lnv_b[:], var_b[:], AF.Ln, bias=0.0,
                                      scale=1.0))
            s.wait_ge(s_s.h, t1)
            s_s.inc(s.activation(rstd_b[:], lnv_b[:], AF.Exp, bias=0.0,
                                 scale=-0.5))
        blk.scalar(_sln)
        stok_rstd = s_s.n

        def _vmrb(v):
            v.wait_ge(s_s.h, stok_rstd)
            s_v.inc(v.scalar_tensor_tensor(mrb_b[:], mu_b[:], -1.0, rstd_b[:],
                                           ALU.mult, ALU.mult))
        blk.vector(_vmrb)
        vtok_mrb = s_v.n

        xn_tok = {}
        x2_tok = {}
        x2_free = {}
        for dc in range(NDC):
            xb = dc % 3

            def _gx2(g, dc=dc, xb=xb):
                if dc < 3:
                    g.wait_ge(s_stat.h, stat_tok[NDC - 1])
                else:
                    g.wait_ge(s_xn.h, x2_free[xb])
                x2_tok[dc] = s_xs[xb].inc(g.dma_start(
                    out=xch[xb][:], in_=x_fm[dc * 128:(dc + 1) * 128, :]), 16)
            blk.gpsimd(_gx2)

            def _vn(v, dc=dc, xb=xb):
                v.wait_ge(s_xs[xb].h, x2_tok[dc])
                if dc == 0:
                    v.wait_ge(s_v.h, vtok_mrb)
                if dc >= 2:
                    v.wait_ge(s_xn.h, xn_tok[dc - 2])
                t1 = s_v.inc(v.tensor_mul(tnc[dc % 2][:], xch[xb][:], rstd_b[:]))
                v.wait_ge(s_v.h, t1)
                xn_tok[dc] = s_xn.inc(v.tensor_add(xnc(dc), tnc[dc % 2][:],
                                                   mrb_b[:]))
                x2_free[xb] = xn_tok[dc]
            blk.vector(_vn)

        projs = [("k", wk_t, cin_k, 1), ("q", wq_t, cin_q, 0),
                 ("v", wv_t, cin_v, 2)]
        w_tok = {}
        wbuf_free = {}
        mm_tok = {}
        ev_tok = {}
        ps_free = {}
        stage_free = {}
        st_mark = {}
        blkno = 0
        for pname, wt, cin, bcol in projs:
            for oc in range(NOC):
                i = blkno % 2
                pb = blkno % 2
                sb = blkno % 2

                def _gw(g, pname=pname, wt=wt, oc=oc, i=i, bn=blkno):
                    if bn >= 2:
                        g.wait_ge(s_mm.h, wbuf_free[i])
                    w_tok[(pname, oc)] = s_ws[i].inc(g.dma_start(
                        out=wbuf[i][:],
                        in_=wt[oc].rearrange("p c j -> p (c j)")), 16)
                blk.gpsimd(_gw)

                def _tp(t, pname=pname, oc=oc, i=i, pb=pb, bn=blkno):
                    if bn == 0:
                        t.wait_ge(s_xn.h, xn_tok[NDC - 1])
                    t.wait_ge(s_ws[i].h, w_tok[(pname, oc)])
                    if pb in ps_free:
                        t.wait_ge(s_ev.h, ps_free[pb])
                    for dc in range(NDC):
                        t.ldweights(wbuf[i][:, dc * 128:(dc + 1) * 128])
                        for tt in range(2):
                            ins = t.matmul(
                                ps_qkv[pb][:, tt * 512:(tt + 1) * 512],
                                wbuf[i][:, dc * 128:(dc + 1) * 128],
                                xnc(dc)[:, tt * 512:(tt + 1) * 512],
                                start=(dc == 0), stop=(dc == NDC - 1))
                    mm_tok[(pname, oc)] = s_mm.inc(ins)
                    wbuf_free[i] = mm_tok[(pname, oc)]
                blk.tensor(_tp)

                def _vev(v, pname=pname, oc=oc, pb=pb, sb=sb, bcol=bcol, bn=blkno):
                    if bn == 0:
                        v.wait_ge(s_in.h, 32)
                    v.wait_ge(s_mm.h, mm_tok[(pname, oc)])
                    if sb in stage_free:
                        v.wait_ge(s_bsts[sb].h, stage_free[sb])
                    col = bcol * NOC + oc
                    ev_tok[(pname, oc)] = s_ev.inc(v.tensor_scalar(
                        stage[sb][:], ps_qkv[pb][:],
                        bias_sb[:, col:col + 1], None, ALU.add))
                    ps_free[pb] = ev_tok[(pname, oc)]
                blk.vector(_vev)

                def _sst(sy, pname=pname, oc=oc, sb=sb, cin=cin):
                    sy.wait_ge(s_ev.h, ev_tok[(pname, oc)])
                    ins = sy.dma_start(out=cin[oc * 128:(oc + 1) * 128, :],
                                       in_=stage[sb][:])
                    stage_free[sb] = s_bsts[sb].inc(ins, 16)
                blk.sync(_sst)
                blkno += 1
            st_mark[pname] = (s_bsts[0].n, s_bsts[1].n)


        cc_tok = {"k": 16, "q": 32, "v": 48}

        def _gdbg(g):
            for name, srcb in (("d_k", cin_k), ("d_q", cin_q),
                               ("d_v", cin_v)):
                if name in dbg:
                    g.wait_ge(s_bsts[0].h, s_bsts[0].n)
                    g.wait_ge(s_bsts[1].h, s_bsts[1].n)
                    s_y.inc(g.dma_start(out=dbg[name][:], in_=srcb[:]), 16)
            if "kg_d" in dbg:
                g.wait_ge(s_cc.h, cc_tok["k"])
                s_y.inc(g.dma_start(out=dbg["kg_d"][:], in_=kg[:]), 16)
        blk.gpsimd(_gdbg)

        if "B" not in phases:
            def _fin(g):
                g.wait_ge(s_bsts[0].h, s_bsts[0].n)
                g.wait_ge(s_bsts[1].h, s_bsts[1].n)
                g.wait_ge(s_cc.h, s_cc.n)
                g.wait_ge(s_y.h, s_y.n)
                s_y.inc(g.dma_start(out=y[0:128, :], in_=stage[0][:]), 16)
                g.wait_ge(s_y.h, s_y.n)
            blk.gpsimd(_fin)
            esA.close()
            return nc

        esA.close()
        return nc
    return nc



def tile_w(w):
    a = np.ascontiguousarray(np.asarray(w).T)
    return np.ascontiguousarray(
        a.reshape(NDC, 128, NOC, 128).transpose(2, 1, 0, 3))


def host_inputs(inputs, c):
    b, p = c // 4, c % 4
    x = np.asarray(inputs["L0_attn_x"])
    x_fm = np.ascontiguousarray(x[b, p * T:(p + 1) * T, :].T).astype(np.float32)
    bias5 = np.concatenate(
        [np.ascontiguousarray(np.asarray(inputs[k]).reshape(NOC, 128).T)
         for k in ("L0_attn_bq", "L0_attn_bk", "L0_attn_bv",
                   "L0_attn_bo", "L0_ffn_base_b")], axis=1).astype(np.float32)
    idx = np.zeros((128, 32), np.int32)
    for r in range(4):
        for g in range(NHL):
            idx[:, r * 8 + g] = r * D + p * 1024 + g * 128 + np.arange(128)
    return {
        "x_fm": x_fm,
        "wq_t": tile_w(inputs["L0_attn_Wq"]).astype(np.float32),
        "wk_t": tile_w(inputs["L0_attn_Wk"]).astype(np.float32),
        "wv_t": tile_w(inputs["L0_attn_Wv"]).astype(np.float32),
        "wo_t": tile_w(inputs["L0_attn_Wo"]).astype(np.float32),
        "wb_t": tile_w(inputs["L0_ffn_base_w"]).astype(np.float32),
        "a_t": np.ascontiguousarray(np.asarray(inputs["L0_ffn_A"]).T
                                    ).reshape(NDC, 128, RK).astype(np.float32),
        "r_t": np.ascontiguousarray(np.asarray(inputs["L0_ffn_router_w"]).T
                                    ).reshape(NDC, 128, NE).astype(np.float32),
        "b_t": np.ascontiguousarray(
            np.asarray(inputs["L0_ffn_B"])
            .transpose(1, 0, 2)
            .reshape(NOC, 128, NE, RK)
            .transpose(0, 2, 3, 1)).astype(np.float32),
        "bias5": bias5,
        "idx_t": idx,
    }



NHA = 16
PAD = [8, 6, 4, 2]
CUM = [0, 8, 14, 18]
TOTK = 20
NIDX = 2 * NHA * TOTK + NHA * 4


def _attn_alloc(nc, e, sems):
    ctx = {}
    ctx["kslab"] = [e(nc.sbuf_tensor(f"kslab{i}", [128, TOTK * 512], BF16))
                    for i in range(2)]
    ctx["vslab"] = [e(nc.sbuf_tensor(f"vslab{i}", [128, TOTK * 512], BF16))
                    for i in range(2)]
    ctx["qslab"] = [e(nc.sbuf_tensor(f"qslab{i}", [128, 2048], BF16)) for i in range(2)]
    ctx["vtm"] = e(nc.sbuf_tensor("vtm", [128, 4096], BF16))
    ctx["scores"] = [e(nc.sbuf_tensor(f"scores{i}", [128, 4096], F32)) for i in range(2)]
    ctx["pbuf"] = [e(nc.sbuf_tensor(f"pbuf{i}", [128, 4096], BF16)) for i in range(2)]
    ctx["ptbuf"] = e(nc.sbuf_tensor("ptbuf", [128, 32 * 512], BF16))
    ctx["rmax"] = e(nc.sbuf_tensor("rmax", [128, 2], F32))
    ctx["rbias"] = e(nc.sbuf_tensor("rbias", [128, 2], F32))
    ctx["rsum"] = e(nc.sbuf_tensor("rsum", [128, 2], F32))
    ctx["rrec"] = e(nc.sbuf_tensor("rrec", [128, 2], F32))
    ctx["aostg"] = [e(nc.sbuf_tensor(f"aostg{i}", [128, 512], F32)) for i in range(2)]
    ctx["aidx"] = e(nc.sbuf_tensor("aidx", [128, NIDX], I32))
    ctx["pmask_sb"] = e(nc.sbuf_tensor("pmask_sb", [128, 4 * 512], F32))
    ctx["aiota"] = e(nc.sbuf_tensor("aiota", [128, 512], F32))
    ctx["amasks"] = e(nc.sbuf_tensor("amasks", [128, 4 * 512], F32))
    ctx["am1t"] = e(nc.sbuf_tensor("am1t", [128, 512], F32))
    ctx["aiden"] = e(nc.sbuf_tensor("aiden", [128, 128], BF16))
    ctx["ps_sc"] = [e(nc.psum_tensor(f"ps_sc{i}", [128, 512], F32)) for i in range(2)]
    ctx["ps_pt"] = [e(nc.psum_tensor(f"ps_pt{i}", [128, 1024], BF16)) for i in range(2)]
    ctx["ps_vt"] = [e(nc.psum_tensor(f"ps_vt{i}", [128, 1024], BF16)) for i in range(2)]
    ctx["ps_av"] = [e(nc.psum_tensor(f"ps_av{i}", [128, 512], F32)) for i in range(2)]
    for n in ["s_ain", "s_am", "s_av2", "s_as", "s_at", "s_scev", "s_scevs",
              "s_ptev", "s_vtev", "s_aoev", "s_soft"]:
        sems[n] = S(nc, e, n)
    for nm in ["g_k", "g_v", "g_q", "st_ao"]:
        sems[nm] = [S(nc, e, f"{nm}{i}") for i in range(2)]


def _emit_attn_consts(nc, blk, ctx, sems):
    s_am = sems["s_am"]
    s_v = sems["s_av2"]

    def _g0(g):
        s_am.inc(g.iota(ctx["aiota"][:], pattern=[[1, 512]], base=0,
                        channel_multiplier=-1,
                        allow_small_or_imprecise_dtypes=True))
    blk.gpsimd(_g0)

    def _v0(v):
        s_am.wait(v, 1)
        s_v.inc(v.tensor_scalar(ctx["aiden"][:], ctx["aiota"][:, 0:128],
                                0.0, None, ALU.is_equal))
        for qb in range(4):
            t1 = s_v.inc(v.tensor_scalar(ctx["am1t"][:], ctx["aiota"][:],
                                         float(-qb * 128), 0.0,
                                         ALU.add, ALU.is_gt))
            v.wait_ge(s_v.h, t1)
            t2 = s_v.inc(v.tensor_scalar(
                ctx["amasks"][:, qb * 512:(qb + 1) * 512], ctx["am1t"][:],
                NEG_BIG, None, ALU.mult))
            v.wait_ge(s_v.h, t2)
    blk.vector(_v0)
    ctx["aconsts_tok"] = s_v.n


def _emit_attention(nc, blk, ctx, sems, kbig, vbig, qbig, idxk, pmask,
                    ao_write, pre_wait=None):
    s_ain = sems["s_ain"]; s_v = sems["s_av2"]; s_s = sems["s_as"]
    s_t = sems["s_at"]; s_scev = sems["s_scev"]; s_scevs = sems["s_scevs"]
    s_ptev = sems["s_ptev"]
    s_vtev = sems["s_vtev"]; s_aoev = sems["s_aoev"]; s_soft = sems["s_soft"]
    g_k = sems["g_k"]; g_v = sems["g_v"]; g_q = sems["g_q"]
    st_ao = sems["st_ao"]
    kslab, vslab, qslab = ctx["kslab"], ctx["vslab"], ctx["qslab"]
    vtm, scores, pbuf, ptbuf = ctx["vtm"], ctx["scores"], ctx["pbuf"], ctx["ptbuf"]
    ps_sc, ps_pt, ps_vt, ps_av = ctx["ps_sc"], ctx["ps_pt"], ctx["ps_vt"], ctx["ps_av"]
    aostg = ctx["aostg"]

    def _sin(sy):
        s_ain.inc(sy.dma_start(out=ctx["aidx"][:], in_=idxk[:]), 16)
        s_ain.inc(sy.dma_start(out=ctx["pmask_sb"][:], in_=pmask[:]), 16)
    blk.sync(_sin)

    def kcol(h, slot, j):
        return h * TOTK + CUM[slot] + j

    def vcol(h, slot, j):
        return NHA * TOTK + h * TOTK + CUM[slot] + j

    def qcol(h, slot):
        return 2 * NHA * TOTK + h * 4 + slot

    kq_free = {}
    v_free = {}
    gk_tok = {}
    gv_tok = {}
    gq_tok = {}

    def emit_gathers(h):
        b = h % 2

        def _g(g, h=h, b=b):
            if h == 0:
                s_ain.wait(g)
                if pre_wait is not None:
                    pre_wait(g)
            if b in kq_free:
                g.wait_ge(s_t.h, kq_free[b])
            if b in v_free:
                g.wait_ge(s_t.h, v_free[b])
            for slot in range(4):
                for j in range(PAD[slot]):
                    c = CUM[slot] + j
                    ins = g.indirect_dma_start(
                        kslab[b][:, c * 512:(c + 1) * 512], None,
                        kbig[:],
                        IndirectOffsetOnAxis(
                            ap=ctx["aidx"][:, kcol(h, slot, j):kcol(h, slot, j) + 1],
                            axis=0))
                    gk_tok[h] = g_k[b].inc(ins, 16)
                    ins = g.indirect_dma_start(
                        vslab[b][:, c * 512:(c + 1) * 512], None,
                        vbig[:],
                        IndirectOffsetOnAxis(
                            ap=ctx["aidx"][:, vcol(h, slot, j):vcol(h, slot, j) + 1],
                            axis=0))
                    gv_tok[h] = g_v[b].inc(ins, 16)
                ins = g.indirect_dma_start(
                    qslab[b][:, slot * 512:(slot + 1) * 512], None,
                    qbig[:],
                    IndirectOffsetOnAxis(
                        ap=ctx["aidx"][:, qcol(h, slot):qcol(h, slot) + 1],
                        axis=0))
                gq_tok[h] = g_q[b].inc(ins, 16)
        blk.gpsimd(_g)

    vt_ev = {}
    pt_ev = {}
    sc_ev = {}
    sb_free = {}
    pbuf_free = {}
    rbias_free = {}
    av_ev = {}
    stg_free = {}
    last_av = [0]
    vtm_war = {}

    emit_gathers(0)
    for h in range(NHA):
        b = h % 2
        if h + 1 < NHA:
            emit_gathers(h + 1)

        for slot in range(4):
            P = PAD[slot]
            nk = P * 4
            base = CUM[slot] * 512

            ngrp = (nk + 7) // 8
            for grp in range(ngrp):
                k0, k1 = grp * 8, min(grp * 8 + 8, nk)
                pbank = grp % 2

                def _tv(t, h=h, b=b, slot=slot, base=base, k0=k0, k1=k1,
                        pbank=pbank, grp=grp):
                    if h == 0 and slot == 0 and grp == 0:
                        t.wait_ge(s_v.h, ctx["aconsts_tok"])
                    t.wait_ge(g_v[b].h, gv_tok[h])
                    if pbank in vt_ev:
                        t.wait_ge(s_vtev.h, vt_ev[pbank])
                    for kc in range(k0, k1):
                        ins = t.transpose(
                            ps_vt[pbank][:, (kc - k0) * 128:(kc - k0 + 1) * 128],
                            vslab[b][:, base + kc * 128:base + (kc + 1) * 128],
                            ctx["aiden"][:])
                    v_free[b] = s_t.inc(ins)
                blk.tensor(_tv)
                mm_tok = sems["s_at"].n

                def _vev(v, k0=k0, k1=k1, pbank=pbank, tok=mm_tok):
                    v.wait_ge(s_t.h, tok)
                    if 0 in vtm_war:
                        v.wait_ge(s_t.h, vtm_war[0])
                    vt_ev[pbank] = s_vtev.inc(v.tensor_copy(
                        vtm[:, k0 * 128:k1 * 128],
                        ps_vt[pbank][:, 0:(k1 - k0) * 128]))
                blk.vector(_vev)

            for qb in range(4):
                sbb = qb % 2
                for j in range(P):
                    scb = j % 2

                    def _ts(t, h=h, b=b, slot=slot, qb=qb, j=j, scb=scb):
                        if j == 0:
                            t.wait_ge(g_k[b].h, gk_tok[h])
                            t.wait_ge(g_q[b].h, gq_tok[h])
                            t.ldweights(qslab[b][:, slot * 512 + qb * 128:
                                                 slot * 512 + (qb + 1) * 128])
                        if scb in sc_ev:
                            _s, _tk = sc_ev[scb]
                            t.wait_ge(_s.h, _tk)
                        ins = t.matmul(ps_sc[scb][:],
                                       qslab[b][:, slot * 512 + qb * 128:
                                                slot * 512 + (qb + 1) * 128],
                                       kslab[b][:, (CUM[slot] + j) * 512:
                                                (CUM[slot] + j + 1) * 512],
                                       start=True, stop=True)
                        kq_free[b] = s_t.inc(ins)
                    blk.tensor(_ts)
                    mm_tok = sems["s_at"].n

                    if j == 0 or j == P - 1:
                        def _vevs(v, slot=slot, qb=qb, j=j, scb=scb, sbb=sbb,
                                  tok=mm_tok):
                            v.wait_ge(s_t.h, tok)
                            if sbb in sb_free and j == 0:
                                v.wait_ge(s_soft.h, sb_free[sbb])
                            mask = (ctx["pmask_sb"][:, slot * 512:(slot + 1) * 512]
                                    if j == 0 else
                                    ctx["amasks"][:, qb * 512:(qb + 1) * 512])
                            sc_ev[scb] = (s_scev, s_scev.inc(v.tensor_tensor(
                                scores[sbb][:, j * 512:(j + 1) * 512],
                                ps_sc[scb][:], mask, ALU.add)))
                        blk.vector(_vevs)
                    else:
                        def _sevs(s, j=j, scb=scb, sbb=sbb, tok=mm_tok,
                                  need_free=(j == 0)):
                            s.wait_ge(s_t.h, tok)
                            sc_ev[scb] = (s_scevs, s_scevs.inc(s.copy(
                                scores[sbb][:, j * 512:(j + 1) * 512],
                                ps_sc[scb][:])))
                        blk.scalar(_sevs)

                ev_tok = s_scev.n
                ev_tok_s = s_scevs.n

                def _vsm(v, sbb=sbb, P=P, ev_tok=ev_tok, ev_tok_s=ev_tok_s,
                         rb_tok=rbias_free.get(sbb, 0)):
                    if ev_tok:
                        v.wait_ge(s_scev.h, ev_tok)
                    if ev_tok_s:
                        v.wait_ge(s_scevs.h, ev_tok_s)
                    if rb_tok:
                        v.wait_ge(s_s.h, rb_tok)
                    t1 = s_v.inc(v.reduce_max(ctx["rmax"][:, sbb:sbb + 1],
                                              scores[sbb][:, 0:P * 512],
                                              axis=AX))
                    v.wait_ge(s_v.h, t1)
                    s_v.inc(v.tensor_scalar(ctx["rbias"][:, sbb:sbb + 1],
                                            ctx["rmax"][:, sbb:sbb + 1],
                                            -SM_SCALE, None, ALU.mult))
                blk.vector(_vsm)
                bias_tok = s_v.n

                def _ssm(s, sbb=sbb, P=P, bias_tok=bias_tok,
                         pb_tok=pbuf_free.get(sbb, 0)):
                    s.wait_ge(s_v.h, bias_tok)
                    if pb_tok:
                        s.wait_ge(s_t.h, pb_tok)
                    s_s.inc(s.activation(pbuf[sbb][:, 0:P * 512],
                                         scores[sbb][:, 0:P * 512], AF.Exp,
                                         bias=ctx["rbias"][:, sbb:sbb + 1],
                                         scale=SM_SCALE,
                                         accum_out=ctx["rsum"][:, sbb:sbb + 1]))
                blk.scalar(_ssm)
                exp_tok = s_s.n
                rbias_free[sbb] = exp_tok

                def _vps(v, sbb=sbb, P=P, exp_tok=exp_tok):
                    v.wait_ge(s_s.h, exp_tok)
                    t1 = s_v.inc(v.reciprocal(ctx["rrec"][:, sbb:sbb + 1],
                                              ctx["rsum"][:, sbb:sbb + 1]))
                    v.wait_ge(s_v.h, t1)
                    s_soft.inc(v.tensor_scalar(pbuf[sbb][:, 0:P * 512],
                                               pbuf[sbb][:, 0:P * 512],
                                               ctx["rrec"][:, sbb:sbb + 1],
                                               None, ALU.mult))
                    sb_free[sbb] = s_soft.n
                blk.vector(_vps)
                soft_tok = s_soft.n

                ngrp = (nk + 7) // 8
                for grp in range(ngrp):
                    k0, k1 = grp * 8, min(grp * 8 + 8, nk)
                    pbank = grp % 2

                    def _tp(t, sbb=sbb, qb=qb, k0=k0, k1=k1, pbank=pbank,
                            soft_tok=soft_tok):
                        t.wait_ge(s_soft.h, soft_tok)
                        if ("pt", pbank) in pt_ev:
                            t.wait_ge(s_ptev.h, pt_ev[("pt", pbank)])
                        for kc in range(k0, k1):
                            ins = t.transpose(
                                ps_pt[pbank][:, (kc - k0) * 128:(kc - k0 + 1) * 128],
                                pbuf[sbb][:, kc * 128:(kc + 1) * 128],
                                ctx["aiden"][:])
                        pbuf_free[sbb] = s_t.inc(ins)
                    blk.tensor(_tp)
                    mm_tok = sems["s_at"].n

                    def _vevp(v, qb=qb, k0=k0, k1=k1, pbank=pbank, tok=mm_tok,
                              av_tok=last_av[0]):
                        v.wait_ge(s_t.h, tok)
                        if av_tok:
                            v.wait_ge(s_t.h, av_tok)
                        for kc in range(k0, k1):
                            ins = v.tensor_copy(
                                ptbuf[:, kc * 512 + qb * 128:kc * 512 + (qb + 1) * 128],
                                ps_pt[pbank][:, (kc - k0) * 128:(kc - k0 + 1) * 128])
                        pt_ev[("pt", pbank)] = s_ptev.inc(ins)
                    blk.vector(_vevp)

            pt_tok = s_ptev.n
            avb = slot % 2

            def _tav(t, h=h, slot=slot, nk=nk, base=base, avb=avb,
                     pt_tok=pt_tok):
                t.wait_ge(s_ptev.h, pt_tok)
                t.wait_ge(s_vtev.h, s_vtev.n)
                if avb in av_ev:
                    t.wait_ge(s_aoev.h, av_ev[avb])
                for kc in range(nk):
                    t.ldweights(vtm[:, kc * 128:(kc + 1) * 128])
                    ins = t.matmul(ps_av[avb][:],
                                   vtm[:, kc * 128:(kc + 1) * 128],
                                   ptbuf[:, kc * 512:(kc + 1) * 512],
                                   start=(kc == 0), stop=(kc == nk - 1))
                s_t.inc(ins)
                last_av[0] = s_t.n
                vtm_war[0] = s_t.n
            blk.tensor(_tav)
            av_tok = sems["s_at"].n

            sgb = (h * 4 + slot) % 2

            def _vao(v, avb=avb, sgb=sgb, av_tok=av_tok):
                v.wait_ge(s_t.h, av_tok)
                if sgb in stg_free:
                    v.wait_ge(st_ao[sgb].h, stg_free[sgb])
                av_ev[avb] = s_aoev.inc(v.tensor_copy(aostg[sgb][:],
                                                      ps_av[avb][:]))
            blk.vector(_vao)
            ao_tok = s_aoev.n

            def _sao(sy, h=h, slot=slot, sgb=sgb, ao_tok=ao_tok):
                sy.wait_ge(s_aoev.h, ao_tok)
                stg_free[sgb] = ao_write(sy, h, slot, aostg[sgb], st_ao[sgb])
            blk.sync(_sao)


def build_attn_nc():
    nc = bass.Bass("TRN2", target_bir_lowering=False, debug=False)
    kbig = nc.dram_tensor("kbig", [24704, 512], BF16, kind="ExternalInput").ap()
    vbig = nc.dram_tensor("vbig", [24704, 512], BF16, kind="ExternalInput").ap()
    qbig = nc.dram_tensor("qbig", [16384, 512], BF16, kind="ExternalInput").ap()
    idxk = nc.dram_tensor("idxk", [128, NIDX], I32, kind="ExternalInput").ap()
    pmask = nc.dram_tensor("pmask", [128, 4 * 512], F32, kind="ExternalInput").ap()
    ao = nc.dram_tensor("ao", [NHA * 128, 2048], F32, kind="ExternalOutput").ap()

    es = ExitStack()
    e = es.enter_context
    with es:
        sems = {}
        ctx = {}
        _attn_alloc_into(nc, e, sems, ctx)
        blk = e(nc.Block())
        _emit_attn_consts(nc, blk, ctx, sems)

        def ao_write(sy, h, slot, stg, sem):
            ins = sy.dma_start(
                out=ao[h * 128:(h + 1) * 128, slot * 512:(slot + 1) * 512],
                in_=stg[:])
            return sem.inc(ins, 16)

        _emit_attention(nc, blk, ctx, sems, kbig, vbig, qbig, idxk, pmask,
                        ao_write)

        def _fin(g):
            for s in sems["st_ao"]:
                g.wait_ge(s.h, s.n)
        blk.gpsimd(_fin)
    return nc


def _attn_alloc_into(nc, e, sems, ctx):
    ctx["kslab"] = [e(nc.sbuf_tensor(f"kslab{i}", [128, TOTK * 512], BF16))
                    for i in range(2)]
    ctx["vslab"] = [e(nc.sbuf_tensor(f"vslab{i}", [128, TOTK * 512], BF16))
                    for i in range(2)]
    ctx["qslab"] = [e(nc.sbuf_tensor(f"qslab{i}", [128, 2048], BF16)) for i in range(2)]
    ctx["vtm"] = e(nc.sbuf_tensor("vtm", [128, 4096], BF16))
    ctx["scores"] = [e(nc.sbuf_tensor(f"scores{i}", [128, 4096], F32)) for i in range(2)]
    ctx["pbuf"] = [e(nc.sbuf_tensor(f"pbuf{i}", [128, 4096], BF16)) for i in range(2)]
    ctx["ptbuf"] = e(nc.sbuf_tensor("ptbuf", [128, 32 * 512], BF16))
    ctx["rmax"] = e(nc.sbuf_tensor("rmax", [128, 2], F32))
    ctx["rbias"] = e(nc.sbuf_tensor("rbias", [128, 2], F32))
    ctx["rsum"] = e(nc.sbuf_tensor("rsum", [128, 2], F32))
    ctx["rrec"] = e(nc.sbuf_tensor("rrec", [128, 2], F32))
    ctx["aostg"] = [e(nc.sbuf_tensor(f"aostg{i}", [128, 512], F32)) for i in range(2)]
    ctx["aidx"] = e(nc.sbuf_tensor("aidx", [128, NIDX], I32))
    ctx["pmask_sb"] = e(nc.sbuf_tensor("pmask_sb", [128, 4 * 512], F32))
    ctx["aiota"] = e(nc.sbuf_tensor("aiota", [128, 512], F32))
    ctx["amasks"] = e(nc.sbuf_tensor("amasks", [128, 4 * 512], F32))
    ctx["am1t"] = e(nc.sbuf_tensor("am1t", [128, 512], F32))
    ctx["aiden"] = e(nc.sbuf_tensor("aiden", [128, 128], BF16))
    ctx["ps_sc"] = [e(nc.psum_tensor(f"ps_sc{i}", [128, 512], F32)) for i in range(2)]
    ctx["ps_pt"] = [e(nc.psum_tensor(f"ps_pt{i}", [128, 1024], BF16)) for i in range(2)]
    ctx["ps_vt"] = [e(nc.psum_tensor(f"ps_vt{i}", [128, 1024], BF16)) for i in range(2)]
    ctx["ps_av"] = [e(nc.psum_tensor(f"ps_av{i}", [128, 512], F32)) for i in range(2)]
    for n in ["s_ain", "s_am", "s_av2", "s_as", "s_at", "s_scev", "s_scevs",
              "s_ptev", "s_vtev", "s_aoev", "s_soft"]:
        sems[n] = S(nc, e, n)
    for nm in ["g_k", "g_v", "g_q", "st_ao"]:
        sems[nm] = [S(nc, e, f"{nm}{i}") for i in range(2)]



def build_ffn_nc():
    nc = bass.Bass("TRN2", target_bir_lowering=False, debug=False)
    ao_in = nc.dram_tensor("ao_in", [D, T], F32, kind="ExternalInput").ap()
    x_fm = nc.dram_tensor("x_fm", [D, T], F32, kind="ExternalInput").ap()
    wo_t = nc.dram_tensor("wo_t", [NOC, 128, NDC, 128], F32, kind="ExternalInput").ap()
    wb_t = nc.dram_tensor("wb_t", [NOC, 128, NDC, 128], F32, kind="ExternalInput").ap()
    a_t = nc.dram_tensor("a_t", [NDC, 128, RK], F32, kind="ExternalInput").ap()
    r_t = nc.dram_tensor("r_t", [NDC, 128, NE], F32, kind="ExternalInput").ap()
    b_t = nc.dram_tensor("b_t", [NOC, NE, RK, 128], F32, kind="ExternalInput").ap()
    bias5 = nc.dram_tensor("bias5", [128, 5 * NOC], F32, kind="ExternalInput").ap()
    y = nc.dram_tensor("y", [D, T], F32, kind="ExternalOutput").ap()
    x2sp = nc.dram_tensor("x2sp", [D, T], F32, kind="Internal").ap()

    es = ExitStack()
    e = es.enter_context
    with es:
        sems = {}
        ctx = {}
        blk = _ffn_alloc_into(nc, e, sems, ctx)
        _emit_ffn(nc, blk, ctx, sems, ao_in, x_fm, wo_t, wb_t, a_t, r_t, b_t,
                  bias5, y, x2sp, ao_is_f32=True)
    return nc


def _ffn_alloc_into(nc, e, sems, ctx):
    ctx["shared8"] = e(nc.sbuf_tensor("shared8", [128, NDC * T], BF16))
    ctx["fwbuf"] = [e(nc.sbuf_tensor(f"fwbuf{i}", [128, D], BF16)) for i in range(2)]
    ctx["fstage"] = [e(nc.sbuf_tensor(f"fstage{i}", [128, T], F32)) for i in range(2)]
    ctx["xch"] = [e(nc.sbuf_tensor(f"fxch{i}", [128, T], F32)) for i in range(2)]
    ctx["x2bc"] = [e(nc.sbuf_tensor(f"x2bc{i}", [128, T], BF16)) for i in range(2)]
    ctx["x2rd"] = [e(nc.sbuf_tensor(f"x2rd{i}", [128, T], BF16)) for i in range(3)]
    ctx["fbias"] = e(nc.sbuf_tensor("fbias", [128, 5 * NOC], F32))
    ctx["fones"] = e(nc.sbuf_tensor("fones", [128, 128], BF16))
    ctx["ones1_64"] = e(nc.sbuf_tensor("ones1_64", [1, 64], F32))
    ctx["mu2"] = e(nc.sbuf_tensor("mu2", [128, T], F32))
    ctx["msq2"] = e(nc.sbuf_tensor("msq2", [128, T], F32))
    ctx["nmsq2"] = e(nc.sbuf_tensor("nmsq2", [128, T], F32))
    ctx["var2"] = e(nc.sbuf_tensor("var2", [128, T], F32))
    ctx["lnv2"] = e(nc.sbuf_tensor("lnv2", [128, T], F32))
    ctx["rstd2"] = e(nc.sbuf_tensor("rstd2", [128, T], F32))
    ctx["mrb2"] = e(nc.sbuf_tensor("mrb2", [128, T], F32))
    ctx["tn2"] = [e(nc.sbuf_tensor(f"tn2_{i}", [128, T], BF16)) for i in range(2)]
    ctx["rh"] = e(nc.sbuf_tensor("rh", [64, T], BF16))
    ctx["rhe"] = e(nc.sbuf_tensor("rhe", [64, NE * T], BF16))
    ctx["cbsb"] = e(nc.sbuf_tensor("cbsb", [64, T], F32))
    ctx["logits"] = e(nc.sbuf_tensor("logits", [128, 8 * NE], F32))
    ctx["e1"] = e(nc.sbuf_tensor("fe1", [128, 8], F32))
    ctx["e2"] = e(nc.sbuf_tensor("fe2", [128, 8], F32))
    ctx["mask1"] = e(nc.sbuf_tensor("fmask1", [128, NE], F32))
    ctx["mask2"] = e(nc.sbuf_tensor("fmask2", [128, NE], F32))
    ctx["lg2"] = e(nc.sbuf_tensor("flg2", [128, NE], F32))
    ctx["ssum"] = e(nc.sbuf_tensor("fssum", [128, 8], F32))
    ctx["srec"] = e(nc.sbuf_tensor("fsrec", [128, 8], F32))
    ctx["w1n"] = e(nc.sbuf_tensor("fw1n", [128, 8], F32))
    ctx["w2n"] = e(nc.sbuf_tensor("fw2n", [128, 8], F32))
    ctx["comb"] = e(nc.sbuf_tensor("fcomb", [128, 8 * NE], F32))
    ctx["combT"] = e(nc.sbuf_tensor("fcombT", [16, 8 * 128], F32))
    ctx["crow"] = e(nc.sbuf_tensor("fcrow", [1, T], F32))
    ctx["bsb"] = [e(nc.sbuf_tensor(f"fbsb{i}", [64, NE * 128], BF16)) for i in range(2)]
    ctx["asb"] = e(nc.sbuf_tensor("fasb", [128, NDC * RK], BF16))
    ctx["rsb"] = e(nc.sbuf_tensor("frsb", [128, NDC * NE], BF16))
    ctx["negm"] = e(nc.sbuf_tensor("fnegm", [128, 8], F32))
    ctx["E2"] = e(nc.sbuf_tensor("fE2", [128, 8], F32))
    ctx["fiota"] = e(nc.sbuf_tensor("fiota", [128, 128], F32))
    ctx["fiden"] = e(nc.sbuf_tensor("fiden", [128, 128], F32))
    ctx["ps_m"] = [e(nc.psum_tensor(f"fps_m{i}", [128, T], F32)) for i in range(2)]
    ctx["ps_s1"] = e(nc.psum_tensor("fps_s1", [128, T], F32))
    ctx["ps_s2"] = e(nc.psum_tensor("fps_s2", [128, T], F32))
    for n in ["f_in", "f_v", "f_s", "f_t", "f_mm", "f_ev", "f_x2st", "f_y",
              "f_stat", "f_xn", "f_rh", "f_rt", "f_cb", "f_misc", "f_ar",
              "f_cr"]:
        sems[n] = S(nc, e, n)
    for nm in ["f_w", "f_x", "f_x2r", "f_ao", "f_b", "f_st"]:
        sems[nm] = [S(nc, e, f"{nm}{i}") for i in range(2 if nm != "f_x2r" else 3)]
    blk = e(nc.Block())
    return blk


def _emit_ffn(nc, blk, ctx, sems, ao_in, x_fm, wo_t, wb_t, a_t, r_t, b_t,
              bias5, y, x2sp, ao_is_f32=True, pre_wait=None, ao_load_src=None):
    f_in = sems["f_in"]; f_v = sems["f_v"]; f_s = sems["f_s"]; f_t = sems["f_t"]
    f_mm = sems["f_mm"]; f_ev = sems["f_ev"]; f_x2st = sems["f_x2st"]
    f_y = sems["f_y"]; f_stat = sems["f_stat"]; f_xn = sems["f_xn"]
    f_rh = sems["f_rh"]; f_rt = sems["f_rt"]; f_cb = sems["f_cb"]
    f_misc = sems["f_misc"]; f_ar = sems["f_ar"]; f_cr = sems["f_cr"]
    f_w = sems["f_w"]; f_x = sems["f_x"]; f_x2r = sems["f_x2r"]
    f_ao = sems["f_ao"]; f_b = sems["f_b"]; f_st = sems["f_st"]
    shared8 = ctx["shared8"]
    fwbuf = ctx["fwbuf"]; fstage = ctx["fstage"]; xch = ctx["xch"]
    x2bc = ctx["x2bc"]; x2rd = ctx["x2rd"]
    ps_m = ctx["ps_m"]; ps_s1 = ctx["ps_s1"]; ps_s2 = ctx["ps_s2"]

    def sh8(dc):
        return shared8[:, dc * T:(dc + 1) * T]

    def _sin(sy):
        f_in.inc(sy.dma_start(out=ctx["fbias"][:], in_=bias5[:]), 16)
    blk.sync(_sin)

    def _gio(g):
        f_misc.inc(g.iota(ctx["fiota"][:], pattern=[[1, 128]], base=0,
                          channel_multiplier=-1,
                          allow_small_or_imprecise_dtypes=True))
    blk.gpsimd(_gio)
    iota_tok = f_misc.n

    def _v0(v):
        v.wait_ge(f_misc.h, iota_tok)
        f_misc.inc(v.memset(ctx["fones"][:], 1.0))
        f_misc.inc(v.memset(ctx["ones1_64"][:], 1.0))
        f_misc.inc(v.tensor_scalar(ctx["fiden"][:], ctx["fiota"][:], 0.0,
                                   None, ALU.is_equal))
    blk.vector(_v0)
    fconsts = f_misc.n

    ao_tok = {}

    def _gao(g):
        if pre_wait is not None:
            pre_wait(g)
        for dc in range(NDC):
            src = ao_in[dc * 128:(dc + 1) * 128, :] if ao_load_src is None \
                else ao_load_src(dc)
            ao_tok[dc] = f_ao[dc % 2].inc(g.dma_start(out=sh8(dc), in_=src), 16)
    blk.gpsimd(_gao)

    w_tok = {}
    wfree = {}
    mm_tok = {}
    psfree = {}
    stage_busy = {}
    x_tok = {}
    xfree = {}
    x2b_free = {}
    stat_tok = {}
    ldw_ones = [False]
    for oc in range(NOC):
        wb = oc % 2
        pb = oc % 2
        sgb = oc % 2

        def _gw(g, oc=oc, wb=wb):
            if oc >= 2:
                g.wait_ge(f_mm.h, wfree[wb])
            w_tok[oc] = f_w[wb].inc(g.dma_start(
                out=fwbuf[wb][:], in_=wo_t[oc].rearrange("p c j -> p (c j)")), 16)
        blk.gpsimd(_gw)

        def _gx(sy, oc=oc):
            xb = oc % 2
            if oc >= 2:
                sy.wait_ge(f_ev.h, xfree[xb])
            x_tok[oc] = f_x[xb].inc(sy.dma_start(
                out=xch[xb][:], in_=x_fm[oc * 128:(oc + 1) * 128, :]), 16)
        blk.sync(_gx)

        def _tp(t, oc=oc, wb=wb, pb=pb):
            t.wait_ge(f_w[wb].h, w_tok[oc])
            if oc < 2:
                t.wait_ge(f_ao[0].h, f_ao[0].n)
                t.wait_ge(f_ao[1].h, f_ao[1].n)
            if pb in psfree:
                t.wait_ge(f_ev.h, psfree[pb])
            for dc in range(NDC):
                t.ldweights(fwbuf[wb][:, dc * 128:(dc + 1) * 128])
                for tt in range(2):
                    ins = t.matmul(ps_m[pb][:, tt * 512:(tt + 1) * 512],
                                   fwbuf[wb][:, dc * 128:(dc + 1) * 128],
                                   sh8(dc)[:, tt * 512:(tt + 1) * 512],
                                   start=(dc == 0), stop=(dc == NDC - 1))
            mm_tok[oc] = f_mm.inc(ins)
            wfree[wb] = mm_tok[oc]
        blk.tensor(_tp)

        def _vev(v, oc=oc, pb=pb, sgb=sgb):
            xb = oc % 2
            v.wait_ge(f_mm.h, mm_tok[oc])
            v.wait_ge(f_x[xb].h, x_tok[oc])
            if oc == 0:
                v.wait_ge(f_in.h, 16)
            if sgb in stage_busy:
                v.wait_ge(f_st[sgb].h, stage_busy[sgb])
            t1 = f_v.inc(v.tensor_scalar(
                fstage[sgb][:], ps_m[pb][:],
                ctx["fbias"][:, 3 * NOC + oc:3 * NOC + oc + 1], None, ALU.add))
            v.wait_ge(f_v.h, t1)
            t2 = f_v.inc(v.tensor_add(fstage[sgb][:], fstage[sgb][:], xch[xb][:]))
            psfree[pb] = t2
            xfree[xb] = t2
            v.wait_ge(f_v.h, t2)
            if sgb in x2b_free:
                v.wait_ge(f_stat.h, x2b_free[sgb])
            t3 = f_ev.inc(v.tensor_copy(x2bc[sgb][:], fstage[sgb][:]))
            v.wait_ge(f_ev.h, t3)
            f_ev.inc(v.tensor_mul(ctx["tn2"][sgb][:], x2bc[sgb][:],
                                  x2bc[sgb][:]))
        blk.vector(_vev)
        ev_tok = f_ev.n

        def _tst(t, oc=oc, sgb=sgb, ev_tok=ev_tok):
            if not ldw_ones[0]:
                t.wait_ge(f_misc.h, fconsts)
                ldw_ones[0] = True
            t.wait_ge(f_ev.h, ev_tok)
            t.ldweights(ctx["fones"][:])
            st = oc == 0
            sp = oc == NOC - 1
            for tt in range(2):
                t.matmul(ps_s1[:, tt * 512:(tt + 1) * 512], ctx["fones"][:],
                         x2bc[sgb][:, tt * 512:(tt + 1) * 512],
                         start=st, stop=sp)
            for tt in range(2):
                ins = t.matmul(ps_s2[:, tt * 512:(tt + 1) * 512], ctx["fones"][:],
                               ctx["tn2"][sgb][:, tt * 512:(tt + 1) * 512],
                               start=st, stop=sp)
            stat_tok[oc] = f_stat.inc(ins)
            x2b_free[sgb] = stat_tok[oc]
        blk.tensor(_tst)

        def _sst(sy, oc=oc, sgb=sgb, ev_tok=ev_tok):
            sy.wait_ge(f_ev.h, ev_tok)
            stage_busy[sgb] = f_st[sgb].inc(sy.dma_start(
                out=x2sp[oc * 128:(oc + 1) * 128, :], in_=fstage[sgb][:]), 16)
        blk.sync(_sst)

    last_stat = sems["f_stat"].n

    def _vln2(v):
        v.wait_ge(f_stat.h, last_stat)
        t1 = f_v.inc(v.tensor_scalar(ctx["mu2"][:], ps_s1[:], 1.0 / D, None,
                                     ALU.mult))
        t2 = f_v.inc(v.tensor_scalar(ctx["msq2"][:], ps_s2[:], 1.0 / D, 1e-5,
                                     ALU.mult, ALU.add))
        v.wait_ge(f_v.h, t2)
        t3 = f_v.inc(v.scalar_tensor_tensor(ctx["nmsq2"][:], ctx["mu2"][:],
                                            -1.0, ctx["mu2"][:],
                                            ALU.mult, ALU.mult))
        v.wait_ge(f_v.h, t3)
        f_v.inc(v.tensor_add(ctx["var2"][:], ctx["msq2"][:], ctx["nmsq2"][:]))
    blk.vector(_vln2)
    var2_tok = sems["f_v"].n

    def _sln2(s):
        s.wait_ge(f_v.h, var2_tok)
        t1 = f_s.inc(s.activation(ctx["lnv2"][:], ctx["var2"][:], AF.Ln,
                                  bias=0.0, scale=1.0))
        s.wait_ge(f_s.h, t1)
        f_s.inc(s.activation(ctx["rstd2"][:], ctx["lnv2"][:], AF.Exp,
                             bias=0.0, scale=-0.5))
    blk.scalar(_sln2)
    rstd2_tok = sems["f_s"].n

    def _vmrb2(v):
        v.wait_ge(f_s.h, rstd2_tok)
        f_v.inc(v.scalar_tensor_tensor(ctx["mrb2"][:], ctx["mu2"][:], -1.0,
                                       ctx["rstd2"][:], ALU.mult, ALU.mult))
    blk.vector(_vmrb2)
    mrb2_tok = sems["f_v"].n

    oproj_done = sems["f_mm"].n
    xn_tok = {}
    x2r_tok = {}
    x2r_free = {}
    for dc in range(NDC):
        xb = dc % 3

        def _gx2(g, dc=dc, xb=xb):
            if dc >= 3:
                g.wait_ge(f_xn.h, x2r_free[xb])
            g.wait_ge(f_st[0].h, stage_busy.get(0, 0))
            g.wait_ge(f_st[1].h, stage_busy.get(1, 0))
            x2r_tok[dc] = f_x2r[xb].inc(g.dma_start(
                out=x2rd[xb][:], in_=x2sp[dc * 128:(dc + 1) * 128, :]), 16)
        blk.gpsimd(_gx2)

        def _vn2(v, dc=dc, xb=xb):
            v.wait_ge(f_x2r[xb].h, x2r_tok[dc])
            if dc == 0:
                v.wait_ge(f_v.h, mrb2_tok)
                v.wait_ge(f_mm.h, oproj_done)
            tb = dc % 2
            if dc >= 2:
                v.wait_ge(f_xn.h, xn_tok[dc - 2])
            t1 = f_v.inc(v.tensor_mul(ctx["tn2"][tb][:], x2rd[xb][:],
                                      ctx["rstd2"][:]))
            v.wait_ge(f_v.h, t1)
            xn_tok[dc] = f_xn.inc(v.tensor_add(sh8(dc), ctx["tn2"][tb][:],
                                               ctx["mrb2"][:]))
            x2r_free[xb] = xn_tok[dc]
        blk.vector(_vn2)
    xn_all = sems["f_xn"].n

    def _gar(g):
        f_ar.inc(g.dma_start(out=ctx["asb"][:],
                             in_=a_t.rearrange("c p r -> p c r")), 16)
        f_ar.inc(g.dma_start(out=ctx["rsb"][:],
                             in_=r_t.rearrange("c p e -> p c e")), 16)
    blk.gpsimd(_gar)
    ar_tok = sems["f_ar"].n

    def _trh(t):
        t.wait_ge(f_xn.h, xn_all)
        t.wait_ge(f_ar.h, ar_tok)
        t.wait_ge(f_v.h, var2_tok)
        for dc in range(NDC):
            t.ldweights(ctx["asb"][:, dc * RK:(dc + 1) * RK])
            for tt in range(2):
                ins = t.matmul(ps_s1[0:64, tt * 512:(tt + 1) * 512],
                               ctx["asb"][:, dc * RK:(dc + 1) * RK],
                               sh8(dc)[:, tt * 512:(tt + 1) * 512],
                               start=(dc == 0), stop=(dc == NDC - 1))
        f_rh.inc(ins)
    blk.tensor(_trh)
    rh_mm_tok = sems["f_rh"].n

    def _vrh(v):
        v.wait_ge(f_rh.h, rh_mm_tok)
        f_rh.inc(v.tensor_copy(ctx["rh"][:], ps_s1[0:64, :]))
    blk.vector(_vrh)
    rh_tok = sems["f_rh"].n

    for tb in range(8):
        def _trt(t, tb=tb):
            if tb == 0:
                t.wait_ge(f_ar.h, ar_tok)
                t.wait_ge(f_xn.h, xn_all)
                t.wait_ge(f_v.h, var2_tok)
            for dc in range(NDC):
                t.ldweights(sh8(dc)[:, tb * 128:(tb + 1) * 128])
                ins = t.matmul(ps_s2[0:128, tb * NE:(tb + 1) * NE],
                               sh8(dc)[:, tb * 128:(tb + 1) * 128],
                               ctx["rsb"][:, dc * NE:(dc + 1) * NE],
                               start=(dc == 0), stop=(dc == NDC - 1))
            f_rt.inc(ins)
        blk.tensor(_trt)
    lg_tok = sems["f_rt"].n

    def _vrt(v, lg_tok=lg_tok):
        v.wait_ge(f_rt.h, lg_tok)
        f_rt.inc(v.tensor_copy(ctx["logits"][:], ps_s2[0:128, 0:8 * NE]))
    blk.vector(_vrt)
    logits_tok = sems["f_rt"].n


    last_stop = [0]
    for tb in range(8):
        def _vtop(v, tb=tb, ls=None):
            if tb == 0:
                v.wait_ge(f_rt.h, logits_tok)
            else:
                v.wait_ge(f_s.h, last_stop[0])
            lg = ctx["logits"][:, tb * NE:(tb + 1) * NE]
            t1 = f_v.inc(v.reduce_max(ctx["e1"][:, tb:tb + 1], lg, axis=AX))
            v.wait_ge(f_v.h, t1)
            t2 = f_v.inc(v.tensor_scalar(ctx["mask1"][:], lg,
                                         ctx["e1"][:, tb:tb + 1], None,
                                         ALU.is_equal))
            t2b = f_v.inc(v.tensor_scalar(ctx["negm"][:, tb:tb + 1],
                                          ctx["e1"][:, tb:tb + 1], -1.0, None,
                                          ALU.mult))
            v.wait_ge(f_v.h, t2b)
            t3 = f_v.inc(v.scalar_tensor_tensor(ctx["lg2"][:], ctx["mask1"][:],
                                                NEG_BIG, lg, ALU.mult, ALU.add))
            v.wait_ge(f_v.h, t3)
            t4 = f_v.inc(v.reduce_max(ctx["e2"][:, tb:tb + 1], ctx["lg2"][:],
                                      axis=AX))
            v.wait_ge(f_v.h, t4)
            f_v.inc(v.tensor_scalar(ctx["mask2"][:], ctx["lg2"][:],
                                    ctx["e2"][:, tb:tb + 1], None,
                                    ALU.is_equal))
        blk.vector(_vtop)
        vt_tok = sems["f_v"].n

        def _stop(s, tb=tb, vt_tok=vt_tok):
            s.wait_ge(f_v.h, vt_tok)
            lg = ctx["logits"][:, tb * NE:(tb + 1) * NE]
            t1 = f_s.inc(s.activation(ctx["lg2"][:], lg, AF.Exp,
                                      bias=ctx["negm"][:, tb:tb + 1],
                                      scale=1.0,
                                      accum_out=ctx["ssum"][:, tb:tb + 1]))
            s.wait_ge(f_s.h, t1)
            f_s.inc(s.activation(ctx["E2"][:, tb:tb + 1],
                                 ctx["e2"][:, tb:tb + 1], AF.Exp,
                                 bias=ctx["negm"][:, tb:tb + 1], scale=1.0))
        blk.scalar(_stop)
        st_tok = sems["f_s"].n
        last_stop[0] = st_tok

        def _vw(v, tb=tb, st_tok=st_tok):
            v.wait_ge(f_s.h, st_tok)
            t1 = f_v.inc(v.tensor_scalar(ctx["ssum"][:, tb:tb + 1],
                                         ctx["ssum"][:, tb:tb + 1], 1e-6, 1.0,
                                         ALU.mult, ALU.add))
            v.wait_ge(f_v.h, t1)
            t2 = f_v.inc(v.tensor_add(ctx["ssum"][:, tb:tb + 1],
                                      ctx["ssum"][:, tb:tb + 1],
                                      ctx["E2"][:, tb:tb + 1]))
            v.wait_ge(f_v.h, t2)
            t3 = f_v.inc(v.reciprocal(ctx["w1n"][:, tb:tb + 1],
                                      ctx["ssum"][:, tb:tb + 1]))
            v.wait_ge(f_v.h, t3)
            t4 = f_v.inc(v.tensor_mul(ctx["w2n"][:, tb:tb + 1],
                                      ctx["E2"][:, tb:tb + 1],
                                      ctx["w1n"][:, tb:tb + 1]))
            v.wait_ge(f_v.h, t4)
            t5 = f_v.inc(v.tensor_scalar(ctx["mask1"][:], ctx["mask1"][:],
                                         ctx["w1n"][:, tb:tb + 1], 0.5,
                                         ALU.mult, ALU.mult))
            t6 = f_v.inc(v.tensor_scalar(ctx["mask2"][:], ctx["mask2"][:],
                                         ctx["w2n"][:, tb:tb + 1], 0.5,
                                         ALU.mult, ALU.mult))
            v.wait_ge(f_v.h, t6)
            f_cb.inc(v.tensor_add(ctx["comb"][:, tb * NE:(tb + 1) * NE],
                                  ctx["mask1"][:], ctx["mask2"][:]))
        blk.vector(_vw)
    comb_tok = sems["f_cb"].n

    for tb in range(8):
        def _tct(t, tb=tb):
            if tb == 0:
                t.wait_ge(f_cb.h, comb_tok)
                t.wait_ge(f_rt.h, logits_tok)
            ins = t.transpose(ps_s2[0:16, tb * 128:(tb + 1) * 128],
                              ctx["comb"][:, tb * NE:(tb + 1) * NE],
                              ctx["fiden"][:])
            f_cb.inc(ins)
        blk.tensor(_tct)
    ct_mm = sems["f_cb"].n

    def _vct(v):
        v.wait_ge(f_cb.h, ct_mm)
        f_cb.inc(v.tensor_copy(ctx["combT"][:], ps_s2[0:16, 0:8 * 128]))
    blk.vector(_vct)
    ctev = sems["f_cb"].n

    cb_mm_last = [0]
    cb_ev = [0]
    for ee in range(NE):
        def _sct(sy, ee=ee):
            if ee == 0:
                sy.wait_ge(f_cb.h, ctev)
            else:
                sy.wait_ge(f_cb.h, cb_mm_last[0])
            f_cr.inc(sy.dma_start(out=ctx["crow"][:],
                                  in_=ctx["combT"][ee:ee + 1, :]), 16)
        blk.sync(_sct)
        crow_tok = sems["f_cr"].n

        def _tcb(t, ee=ee, crow_tok=crow_tok):
            t.wait_ge(f_cr.h, crow_tok)
            if ee == 0:
                t.wait_ge(f_rh.h, rh_tok)
                t.wait_ge(f_misc.h, fconsts)
            else:
                t.wait_ge(f_cb.h, cb_ev[0])
            for tt in range(2):
                ins = t.matmul(ps_s1[0:64, tt * 512:(tt + 1) * 512],
                               ctx["ones1_64"][:],
                               ctx["crow"][0:1, tt * 512:(tt + 1) * 512],
                               start=True, stop=True)
            f_cb.inc(ins)
            cb_mm_last[0] = sems["f_cb"].n
        blk.tensor(_tcb)
        cb_mm = sems["f_cb"].n

        def _vcb(v, ee=ee, cb_mm=cb_mm):
            v.wait_ge(f_cb.h, cb_mm)
            t1 = f_cb.inc(v.tensor_copy(ctx["cbsb"][:], ps_s1[0:64, :]))
            cb_ev[0] = t1
            v.wait_ge(f_cb.h, t1)
            f_cb.inc(v.tensor_mul(ctx["rhe"][:, ee * T:(ee + 1) * T],
                                  ctx["rh"][:], ctx["cbsb"][:]))
        blk.vector(_vcb)
    rhe_tok = sems["f_cb"].n

    fw2_tok = {}
    fb_tok = {}
    fmm2 = {}
    fw2_free = {}
    fb_free = {}
    ps2_free = {}
    fst2_busy = {}
    yx_tok = {}
    yx_free = {}
    for oc in range(NOC):
        wb = oc % 2
        pb = oc % 2
        sgb = oc % 2

        def _gw2(g, oc=oc, wb=wb):
            if oc >= 2:
                g.wait_ge(f_mm.h, fw2_free[wb])
                g.wait_ge(f_mm.h, fb_free[wb])
            fw2_tok[oc] = f_w[wb].inc(g.dma_start(
                out=ctx["fwbuf"][wb][:],
                in_=wb_t[oc].rearrange("p c j -> p (c j)")), 16)
            fb_tok[oc] = f_b[wb].inc(g.dma_start(
                out=ctx["bsb"][wb][:],
                in_=b_t[oc].rearrange("e r j -> r e j")), 16)
        blk.gpsimd(_gw2)

        def _gx2b(sy, oc=oc):
            xb = oc % 2
            if oc >= 2:
                sy.wait_ge(f_ev.h, yx_free[xb])
            sy.wait_ge(f_st[0].h, stage_busy.get(0, 0))
            sy.wait_ge(f_st[1].h, stage_busy.get(1, 0))
            yx_tok[oc] = f_x[xb].inc(sy.dma_start(
                out=xch[xb][:], in_=x2sp[oc * 128:(oc + 1) * 128, :]), 16)
        blk.sync(_gx2b)

        def _tf(t, oc=oc, wb=wb, pb=pb):
            t.wait_ge(f_w[wb].h, fw2_tok[oc])
            t.wait_ge(f_b[wb].h, fb_tok[oc])
            if oc == 0:
                t.wait_ge(f_cb.h, rhe_tok)
                t.wait_ge(f_xn.h, xn_all)
            if pb in ps2_free:
                t.wait_ge(f_ev.h, ps2_free[pb])
            for dc in range(NDC):
                t.ldweights(ctx["fwbuf"][wb][:, dc * 128:(dc + 1) * 128])
                for tt in range(2):
                    ins = t.matmul(ps_m[pb][:, tt * 512:(tt + 1) * 512],
                                   ctx["fwbuf"][wb][:, dc * 128:(dc + 1) * 128],
                                   sh8(dc)[:, tt * 512:(tt + 1) * 512],
                                   start=(dc == 0), stop=False)
            for ee in range(NE):
                t.ldweights(ctx["bsb"][wb][:, ee * 128:(ee + 1) * 128])
                for tt in range(2):
                    ins = t.matmul(ps_m[pb][:, tt * 512:(tt + 1) * 512],
                                   ctx["bsb"][wb][:, ee * 128:(ee + 1) * 128],
                                   ctx["rhe"][:, ee * T + tt * 512:
                                              ee * T + (tt + 1) * 512],
                                   start=False, stop=(ee == NE - 1))
            fmm2[oc] = f_mm.inc(ins)
            fw2_free[wb] = fmm2[oc]
            fb_free[wb] = fmm2[oc]
        blk.tensor(_tf)

        def _vf(v, oc=oc, pb=pb, sgb=sgb):
            xb = oc % 2
            v.wait_ge(f_mm.h, fmm2[oc])
            v.wait_ge(f_x[xb].h, yx_tok[oc])
            if sgb in fst2_busy:
                v.wait_ge(f_st[sgb].h, fst2_busy[sgb])
            t1 = f_v.inc(v.tensor_scalar(
                fstage[sgb][:], ps_m[pb][:],
                ctx["fbias"][:, 4 * NOC + oc:4 * NOC + oc + 1], None, ALU.add))
            v.wait_ge(f_v.h, t1)
            t2 = f_ev.inc(v.tensor_add(fstage[sgb][:], fstage[sgb][:],
                                       xch[xb][:]))
            ps2_free[pb] = t2
            yx_free[xb] = t2
        blk.vector(_vf)
        evf_tok = sems["f_ev"].n

        def _syf(sy, oc=oc, sgb=sgb, evf_tok=evf_tok):
            sy.wait_ge(f_ev.h, evf_tok)
            fst2_busy[sgb] = f_st[sgb].inc(sy.dma_start(
                out=y[oc * 128:(oc + 1) * 128, :], in_=fstage[sgb][:]), 16)
        blk.sync(_syf)

    def _fin(g):
        g.wait_ge(f_st[0].h, f_st[0].n)
        g.wait_ge(f_st[1].h, f_st[1].n)
    blk.gpsimd(_fin)


SB_PAIR = [[0, 3, 4, 7], [1, 2, 5, 6]]
SLOT_R = [[7, 4, 3, 0], [6, 5, 2, 1]]
ZROW = 24576


def core_tokens(c):
    pp = (c // 2) % 2
    pos = c % 2
    sbs = sorted([SLOT_R[pp][0], SLOT_R[pp][3]] if pos == 0 else
                 [SLOT_R[pp][1], SLOT_R[pp][2]])
    idx = np.concatenate([np.arange(sb * 512, (sb + 1) * 512) for sb in sbs])
    return idx


def host_inputs_a(inputs, c):
    b = c // 4
    x = np.asarray(inputs["L0_attn_x"])
    tk = core_tokens(c)
    x_fm = np.ascontiguousarray(x[b][tk, :].T).astype(np.float32)
    d = host_weights(inputs)
    d["x_fm"] = x_fm
    d["idx_t"] = np.zeros((128, 32), np.int32)
    return d


def host_weights(inputs):
    bias5 = np.concatenate(
        [np.ascontiguousarray(np.asarray(inputs[k]).reshape(NOC, 128).T)
         for k in ("L0_attn_bq", "L0_attn_bk", "L0_attn_bv",
                   "L0_attn_bo", "L0_ffn_base_b")], axis=1).astype(np.float32)
    return {
        "wq_t": tile_w(inputs["L0_attn_Wq"]).astype(np.float32),
        "wk_t": tile_w(inputs["L0_attn_Wk"]).astype(np.float32),
        "wv_t": tile_w(inputs["L0_attn_Wv"]).astype(np.float32),
        "wo_t": tile_w(inputs["L0_attn_Wo"]).astype(np.float32),
        "wb_t": tile_w(inputs["L0_ffn_base_w"]).astype(np.float32),
        "a_t": np.ascontiguousarray(np.asarray(inputs["L0_ffn_A"]).T
                                    ).reshape(NDC, 128, RK).astype(np.float32),
        "r_t": np.ascontiguousarray(np.asarray(inputs["L0_ffn_router_w"]).T
                                    ).reshape(NDC, 128, NE).astype(np.float32),
        "b_t": np.ascontiguousarray(
            np.asarray(inputs["L0_ffn_B"]).transpose(1, 0, 2)
            .reshape(NOC, 128, NE, RK).transpose(0, 2, 3, 1)).astype(np.float32),
        "bias5": bias5,
    }


def host_inputs_b(kfm, vfm, qfm, c):
    import ml_dtypes
    b = c // 4
    pp = (c // 2) % 2
    pos = c % 2
    slotR = SLOT_R[pp]
    oslotR = SLOT_R[1 - pp]
    pairsb = SB_PAIR[pp]

    def mkbig(fm):
        big = np.zeros((24704, 512), ml_dtypes.bfloat16)
        a = fm[b].reshape(4096, 8, 512)
        big[0:16384] = a[:, slotR, :].reshape(16384, 512)
        myd = np.arange(pos * 2048, (pos + 1) * 2048)
        big[16384:16384 + 8192] = a[myd][:, oslotR, :].reshape(8192, 512)
        return big

    kbig = mkbig(kfm)
    vbig = mkbig(vfm)
    qa = qfm[b].reshape(4096, 8, 512)
    qbig = np.ascontiguousarray(
        qa[:, slotR, :].reshape(16384, 512)).astype(ml_dtypes.bfloat16)

    idx = np.zeros((128, NIDX), np.int32)
    ii = np.arange(128)
    for h in range(NHA):
        hg = pos * NHA + h
        for slot in range(4):
            R = slotR[slot]
            P = PAD[slot]
            npad = P - R - 1
            lst_list = [None] * npad + list(range(R)) + [R]
            for j, g in enumerate(lst_list):
                col = h * TOTK + CUM[slot] + j
                if g is None:
                    idx[:, col] = ZROW + ii
                elif g in pairsb:
                    idx[:, col] = (hg * 128 + ii) * 4 + slotR.index(g)
                else:
                    idx[:, col] = 16384 + (h * 128 + ii) * 4 + oslotR.index(g)
                idx[:, NHA * TOTK + col] = idx[:, col]
            idx[:, 2 * NHA * TOTK + h * 4 + slot] = (hg * 128 + ii) * 4 + slot
    pm = np.zeros((128, 4 * 512), np.float32)
    for slot in range(4):
        if PAD[slot] - slotR[slot] - 1 > 0:
            pm[:, slot * 512:(slot + 1) * 512] = NEG_BIG
    return {"kbig": np.ascontiguousarray(kbig),
            "vbig": np.ascontiguousarray(vbig),
            "qbig": qbig, "idxk": idx, "pmask": pm}


def assemble_ao(results):
    attn = [np.zeros((4096, 4096), np.float32) for _ in range(2)]
    for c in range(8):
        b = c // 4
        pp = (c // 2) % 2
        pos = c % 2
        slotR = SLOT_R[pp]
        ao = np.asarray(results[c]["ao"], np.float32)
        for h in range(NHA):
            hg = pos * NHA + h
            for slot in range(4):
                g = slotR[slot]
                attn[b][hg * 128:(hg + 1) * 128, g * 512:(g + 1) * 512] = \
                    ao[h * 128:(h + 1) * 128, slot * 512:(slot + 1) * 512]
    return attn


def kernel_unfused(**inputs):
    inputs = {k: np.asarray(v) for k, v in inputs.items()}
    nc_a = build_nc(phases="A", debug=("d_k", "d_q", "d_v"))
    maps_a = [host_inputs_a(inputs, c) for c in range(8)]
    res_a = run_bass_kernel_spmd(nc_a, maps_a, core_ids=list(range(8)))

    kfm = [np.zeros((4096, 4096), np.float32) for _ in range(2)]
    vfm = [np.zeros((4096, 4096), np.float32) for _ in range(2)]
    qfm = [np.zeros((4096, 4096), np.float32) for _ in range(2)]
    for c in range(8):
        b = c // 4
        tk = core_tokens(c)
        kfm[b][:, tk] = np.asarray(res_a.results[c]["d_k"], np.float32)
        qfm[b][:, tk] = np.asarray(res_a.results[c]["d_q"], np.float32)
        vfm[b][:, tk] = np.asarray(res_a.results[c]["d_v"], np.float32)

    nc_b = build_attn_nc()
    maps_b = [host_inputs_b(kfm, vfm, qfm, c) for c in range(8)]
    res_b = run_bass_kernel_spmd(nc_b, maps_b, core_ids=list(range(8)))
    attn = assemble_ao(res_b.results)

    nc_c = build_ffn_nc()
    wts = host_weights(inputs)
    maps_c = []
    for c in range(8):
        b = c // 4
        tk = core_tokens(c)
        x = np.asarray(inputs["L0_attn_x"])
        d = {"ao_in": np.ascontiguousarray(attn[b][:, tk]).astype(np.float32),
             "x_fm": np.ascontiguousarray(x[b][tk, :].T).astype(np.float32),
             "wo_t": wts["wo_t"], "wb_t": wts["wb_t"], "a_t": wts["a_t"],
             "r_t": wts["r_t"], "b_t": wts["b_t"], "bias5": wts["bias5"]}
        maps_c.append(d)
    res_c = run_bass_kernel_spmd(nc_c, maps_c, core_ids=list(range(8)))

    out = np.zeros((2, 4096, 4096), np.float32)
    for c in range(8):
        b = c // 4
        tk = core_tokens(c)
        out[b][tk, :] = np.asarray(res_c.results[c]["y"], np.float32).T
    return out


def kernel(**inputs):
    return kernel_unfused(**inputs)
